# version 11 (final)
# baseline (speedup 1.0000x reference)
import math
from contextlib import ExitStack

import numpy as np
import ml_dtypes

import concourse.bass as bass
import concourse.mybir as mybir
from concourse.bass_utils import run_bass_kernel_spmd
from concourse.library_overlay import lower_extended_insts

F32 = mybir.dt.float32
BF16 = mybir.dt.bfloat16
ACTF = mybir.ActivationFunctionType
ALU = mybir.AluOpType

D = 1024
HD = 64
NCORES = 8
ENGS = ("PE", "VE", "AC", "GP")
QMAP = {"QS": "QS", "QA": "AC", "QG": "GP"}


def is_queue(sname):
    return "." in sname


def rows4(t, rowbase, colofs, ncols, pitch, lastdim=1):
    return bass.AP(t, rowbase * pitch + colofs,
                   [[64 * pitch, 2], [pitch, 32], [lastdim, ncols]])


class Tracker:
    def __init__(self, nc, es):
        self.nc = nc
        self.es = es
        self.sem = {}
        self.cnt = {}
        self.waited = {}
        self.drained = {}
        self.wr = {}
        self.rd = {}
        for name in ENGS + ("QS",):
            self.sem[name] = es.enter_context(nc.semaphore(name))
            self.cnt[name] = 0
            self.drained[name] = 0

    def ensure(self, sname):
        if sname not in self.sem:
            self.sem[sname] = self.es.enter_context(
                self.nc.semaphore(sname.replace(".", "_")))
            self.cnt[sname] = 0

    def wait(self, eng, ename, sname, n):
        if n <= 0:
            return
        key = (ename, sname)
        if self.waited.get(key, 0) >= n:
            return
        eng.wait_ge(self.sem[sname], n)
        self.waited[key] = n

    def pre(self, eng, ename, reads, writes):
        def _n(sname, c):
            return self.cnt[sname] if is_queue(sname) else c

        for R in reads:
            w = self.wr.get(R)
            if w is None:
                continue
            ws, wc = w
            if ws == ename:
                if wc > self.drained[ename]:
                    eng.drain()
                    self.drained[ename] = self.cnt[ename]
            else:
                self.wait(eng, ename, ws, _n(ws, wc))
        for R in writes:
            for rsname, rc in self.rd.get(R, {}).items():
                if rsname == ename:
                    if rc > self.drained[ename]:
                        eng.drain()
                        self.drained[ename] = self.cnt[ename]
                    continue
                self.wait(eng, ename, rsname, _n(rsname, rc))
            w = self.wr.get(R)
            if w is not None and w[0] != ename:
                self.wait(eng, ename, w[0], _n(w[0], w[1]))

    def post(self, instr, sname, reads, writes, dma):
        k = 16 if dma else 1
        instr.then_inc(self.sem[sname], k)
        self.cnt[sname] += k
        c = self.cnt[sname]
        for R in reads:
            self.rd.setdefault(R, {})[sname] = c
        for R in writes:
            self.wr[R] = (sname, c)

    def mark_read(self, R, sname):
        self.rd.setdefault(R, {})[sname] = self.cnt[sname]

    def barrier_region(self, R):
        for s in list(self.sem):
            self.mark_read(R, s)


def build_program(B, S, dbg=False):
    NST = S // 512
    nc = bass.Bass("TRN2", target_bir_lowering=False)

    x_h = nc.dram_tensor("x", [B * S, D], F32, kind="ExternalInput")
    wq_h = nc.dram_tensor("wq", [D, D], F32, kind="ExternalInput")
    wk_h = nc.dram_tensor("wk", [D, D], F32, kind="ExternalInput")
    wv_h = nc.dram_tensor("wv", [D, D], F32, kind="ExternalInput")
    wo_h = nc.dram_tensor("wo", [D, D], F32, kind="ExternalInput")
    wf_h = nc.dram_tensor("wf", [D, D], F32, kind="ExternalInput")
    ident_h = nc.dram_tensor("identt", [128, 128], BF16, kind="ExternalInput")
    masks_h = nc.dram_tensor("maskst", [128, 4 * 512], BF16, kind="ExternalInput")
    ctab_h = nc.dram_tensor("ctabt", [128, S], BF16, kind="ExternalInput")
    stab_h = nc.dram_tensor("stabt", [128, S], BF16, kind="ExternalInput")
    bq_h = nc.dram_tensor("bqt", [128, 16], F32, kind="ExternalInput")
    bk_h = nc.dram_tensor("bkt", [128, 16], F32, kind="ExternalInput")
    bo_h = nc.dram_tensor("bot", [128, 8], F32, kind="ExternalInput")
    bv_h = nc.dram_tensor("bvr", [128, D], F32, kind="ExternalInput")
    y_h = nc.dram_tensor("y", [B * S, D], F32, kind="ExternalOutput")
    if dbg:
        dbg_x1_h = nc.dram_tensor("dbg_x1", [B * S, D], F32, kind="ExternalOutput")
        dbg_ctxn_h = nc.dram_tensor("dbg_ctxn", [128, B * (S // 512) * 4096], BF16,
                                    kind="ExternalOutput")
    x_ap = x_h.ap()
    y_ap = y_h.ap()

    es = ExitStack()
    with es:
        T = Tracker(nc, es)

        def sb(name, shape, dt, scope=None):
            return (scope or es).enter_context(nc.sbuf_tensor(name, shape, dt))

        wsgnT = sb("wsgnT", [128, 8192], BF16)
        kT = sb("kT", [128, 8 * S], BF16)
        qT = sb("qT", [128, 8 * S], BF16)
        v_sb = sb("v_sb", [128, (S // 128) * D], BF16)
        ident = sb("ident", [128, 128], BF16)
        masks = sb("masks", [128, 4 * 512], BF16)
        bq_sb = sb("bq_sb", [128, 16], F32)
        bk_sb = sb("bk_sb", [128, 16], F32)
        bo_sb = sb("bo_sb", [128, 8], F32)
        bv_rep = sb("bv_rep", [128, D], F32)
        ones_sb = sb("ones_sb", [128, 66], BF16)
        eps_sb = sb("eps_sb", [128, 1], F32)
        xt = sb("xt", [128, 2 * D], F32)
        mv = sb("mv", [128, 8], F32)
        sq = sb("sq", [128, 4], F32)
        rs = sb("rs", [128, 4], F32)
        stats = sb("stats", [128, 12], F32)

        ps_a = es.enter_context(nc.psum_tensor("ps_a", [128, 1024], F32))
        ps_sc = es.enter_context(nc.psum_tensor("ps_sc", [128, 2048], F32))
        ps_c6 = es.enter_context(nc.psum_tensor("ps_c6", [128, 512], F32))
        ps_c7 = es.enter_context(nc.psum_tensor("ps_c7", [128, 512], F32))
        PA = ["pa0", "pa1"]
        SCQ = ["sq0", "sq1", "sq2", "sq3"]

        with nc.Block() as block:
            SECT = {"PE": block.tensor, "VE": block.vector, "AC": block.scalar,
                    "GP": block.gpsimd, "QS": block.sync}

            def op(sname, emit, reads=(), writes=()):
                dma = is_queue(sname)
                if dma:
                    T.ensure(sname)
                    ename = QMAP[sname.split(".")[0]]
                else:
                    ename = sname

                def body(eng):
                    T.pre(eng, ename, reads, writes)
                    instr = emit(eng)
                    T.post(instr, sname, reads, writes, dma)
                SECT[ename if not dma else {"QS": "QS", "AC": "AC", "GP": "GP"}[ename]](body)

            def wload(sname_dst_src):
                for nm, dst, srch in sname_dst_src:
                    op("QG.w", lambda e, dst=dst, srch=srch: e.dma_start(
                        dst[:, :],
                        bass.AP(srch, 0, [[1024, 128], [128 * 1024, 8], [1, 1024]])),
                        writes=[nm])

            def ln_group(tiles, srcs, read_regs):
                for i, t in enumerate(tiles):
                    a, bb = srcs[i]
                    op("VE", lambda e, a=a: e.bn_stats(stats[:, 0:6], a),
                       reads=read_regs[i], writes=["stats"])
                    op("VE", lambda e, bb=bb: e.bn_stats(stats[:, 6:12], bb),
                       reads=read_regs[i], writes=["stats"])
                    op("VE", lambda e, t=t: e.bn_aggr(mv[:, 2 * t:2 * t + 2], stats[:, 0:12]),
                       reads=["stats"], writes=[f"mv{t}"])
                    op("AC", lambda e, t=t: e.activation(
                        sq[:, t:t + 1], mv[:, 2 * t + 1:2 * t + 2], ACTF.Sqrt,
                        bias=eps_sb[:, 0:1]),
                        reads=[f"mv{t}", "eps"], writes=[f"sqr{t}"])
                for t in tiles:
                    op("VE", lambda e, t=t: e.reciprocal_approx_fast(
                        rs[:, t:t + 1], sq[:, t:t + 1]),
                        reads=[f"sqr{t}"], writes=[f"rs{t}"])

            for nm, dst, srch in (("ident", ident, ident_h), ("masks", masks, masks_h),
                                  ("bq", bq_sb, bq_h), ("bk", bk_sb, bk_h),
                                  ("bo", bo_sb, bo_h), ("bv", bv_rep, bv_h)):
                op("QS.c", lambda e, dst=dst, srch=srch: e.dma_start(dst[:, :], srch.ap()[:, :]),
                   writes=[nm])
            op("VE", lambda e: e.memset(ones_sb[:, :], 1.0), writes=["ones2"])
            op("VE", lambda e: e.memset(eps_sb[:, :], 1e-5), writes=["eps"])

            with ExitStack() as s0:
                wf_raw = sb("wf_raw", [128, 8192], BF16, s0)
                T.barrier_region("Wfraw")
                wload([("Wfraw", wf_raw, wf_h)])
                op("AC", lambda e: e.sign(wf_raw[:, :], wf_raw[:, :]),
                   reads=["Wfraw"], writes=["Wfraw"])
                for c in range(8):
                    for m in range(8):
                        half = ((c * 8 + m) // 4) % 2
                        op("PE", lambda e, c=c, m=m, half=half: e.matmul(
                            ps_a[:, half * 512 + (m % 4) * 128: half * 512 + (m % 4) * 128 + 128],
                            wf_raw[:, m * 1024 + c * 128: m * 1024 + c * 128 + 128],
                            ident[:, :], start=True, stop=True),
                            reads=["Wfraw", "ident"], writes=[PA[half]])
                        if m % 4 == 3:
                            op("VE", lambda e, c=c, m=m, half=half: e.tensor_copy(
                                bass.AP(wsgnT, c * 1024 + (m - 3) * 128,
                                        [[8192, 128], [128, 4], [1, 128]]),
                                ps_a[:, half * 512: half * 512 + 512]),
                                reads=[PA[half]], writes=["Wsgn"])

            for b in range(B):
                with ExitStack() as asc:
                    for R in ("Wq", "Wk", "Wv", "xnT", "xn0", "xn1", "t0", "t1",
                              "ctab", "stab"):
                        T.barrier_region(R)
                    wq_sb = sb(f"wq_sb{b}", [128, 8192], BF16, asc)
                    wk_sb = sb(f"wk_sb{b}", [128, 8192], BF16, asc)
                    wv_sb = sb(f"wv_sb{b}", [128, 8192], BF16, asc)
                    xn_st = sb(f"xn_st{b}", [128, 2 * 1024], BF16, asc)
                    xnT = sb(f"xnT{b}", [128, 8 * 512], BF16, asc)
                    t0f = sb(f"t0f{b}", [128, 512], F32, asc)
                    t1f = sb(f"t1f{b}", [128, 512], F32, asc)
                    ctab = sb(f"ctab{b}", [128, S], BF16, asc)
                    stab = sb(f"stab{b}", [128, S], BF16, asc)
                    op("QS.ct", lambda e: e.dma_start(ctab[:, :], ctab_h.ap()[:, :]), writes=["ctab"])
                    op("QS.ct", lambda e: e.dma_start(stab[:, :], stab_h.ap()[:, :]), writes=["stab"])
                    wload([("Wq", wq_sb, wq_h), ("Wk", wk_sb, wk_h), ("Wv", wv_sb, wv_h)])

                    for st in range(NST):
                        for p in range(2):
                            tls = [2 * p, 2 * p + 1]
                            for t in tls:
                                r0 = b * S + (st * 4 + t) * 128
                                op(f"QS.x{t % 2}", lambda e, t=t, r0=r0: e.dma_start(
                                    xt[:, (t % 2) * D:(t % 2) * D + D], x_ap[r0:r0 + 128, :]),
                                    writes=[f"xt{t % 2}"])
                            srcs = [(xt[:, (t % 2) * D:(t % 2) * D + 512],
                                     xt[:, (t % 2) * D + 512:(t % 2) * D + D]) for t in tls]
                            ln_group(tls, srcs, [[f"xt{t % 2}"] for t in tls])
                            for t in tls:
                                op("VE", lambda e, t=t: e.tensor_scalar(
                                    xn_st[:, (t % 2) * D:(t % 2) * D + D],
                                    xt[:, (t % 2) * D:(t % 2) * D + D],
                                    mv[:, 2 * t:2 * t + 1], rs[:, t:t + 1],
                                    ALU.subtract, ALU.mult),
                                    reads=[f"xt{t % 2}", f"mv{t}", f"rs{t}"],
                                    writes=[f"xn{t % 2}"])
                            for t in tls:
                                regs = PA if t % 2 == 0 else SCQ[:2]
                                for c in range(8):
                                    op("PE", lambda e, t=t, c=c: e.matmul(
                                        (ps_a if t % 2 == 0 else ps_sc)[:, c * 128:(c + 1) * 128],
                                        xn_st[:, (t % 2) * D + c * 128:(t % 2) * D + c * 128 + 128],
                                        ident[:, :], start=True, stop=True),
                                        reads=[f"xn{t % 2}", "ident"], writes=regs)
                                op("VE", lambda e, t=t: e.tensor_copy(
                                    bass.AP(xnT, t * 128, [[4096, 128], [512, 8], [1, 128]]),
                                    (ps_a if t % 2 == 0 else ps_sc)[:, 0:1024]),
                                    reads=regs, writes=["xnT"])

                        for which, wsb, wnm, btab, bnm, dstT in (
                                ("q", wq_sb, "Wq", bq_sb, "bq", qT),
                                ("k", wk_sb, "Wk", bk_sb, "bk", kT)):
                            for hp in range(8):
                                slot = hp % 2
                                qreg = SCQ[2 + slot]
                                pofs = 1024 + slot * 512
                                for c in range(8):
                                    op("PE", lambda e, hp=hp, c=c, wsb=wsb, pofs=pofs: e.matmul(
                                        ps_sc[:, pofs:pofs + 512],
                                        wsb[:, c * 1024 + hp * 128: c * 1024 + hp * 128 + 128],
                                        xnT[:, c * 512:(c + 1) * 512],
                                        start=(c == 0), stop=(c == 7)),
                                        reads=[wnm, "xnT"], writes=[qreg])
                                dc = hp * S + st * 512
                                cs0 = st * 512
                                op("VE", lambda e, pofs=pofs, hp=hp, btab=btab, cs0=cs0:
                                   e.scalar_tensor_tensor(
                                       t0f[:, :], ps_sc[:, pofs:pofs + 512],
                                       btab[:, hp:hp + 1],
                                       ctab[:, cs0:cs0 + 512], ALU.add, ALU.mult),
                                   reads=[qreg, bnm, "ctab"], writes=["t0"])
                                for (o_r, i_r) in ((0, 32), (64, 96), (32, 0), (96, 64)):
                                    op("VE", lambda e, pofs=pofs, hp=hp, btab=btab,
                                       cs0=cs0, o_r=o_r, i_r=i_r:
                                       e.scalar_tensor_tensor(
                                           t1f[o_r:o_r + 32, :],
                                           ps_sc[i_r:i_r + 32, pofs:pofs + 512],
                                           btab[o_r:o_r + 32, 8 + hp:9 + hp],
                                           stab[o_r:o_r + 32, cs0:cs0 + 512],
                                           ALU.add, ALU.mult),
                                       reads=[qreg, bnm, "stab"], writes=["t1"])
                                for o_r in (0, 64, 32, 96):
                                    alu = ALU.subtract if o_r in (0, 64) else ALU.add
                                    op("VE", lambda e, dc=dc, dstT=dstT, o_r=o_r, alu=alu:
                                       e.tensor_tensor(
                                           dstT[o_r:o_r + 32, dc:dc + 512],
                                           t0f[o_r:o_r + 32, :],
                                           t1f[o_r:o_r + 32, :], alu),
                                       reads=["t0", "t1", qreg], writes=[which + "T"])

                        for t in range(4):
                            tt = st * 4 + t
                            for half in range(2):
                                slot = (t * 2 + half) % 2
                                vreg = "pc6" if slot == 0 else "pc7"
                                dstp = ps_c6 if slot == 0 else ps_c7
                                for c in range(8):
                                    op("PE", lambda e, t=t, c=c, half=half, dstp=dstp: e.matmul(
                                        dstp[:, :],
                                        xnT[:, c * 512 + t * 128: c * 512 + t * 128 + 128],
                                        wv_sb[:, c * 1024 + half * 512: c * 1024 + (half + 1) * 512],
                                        start=(c == 0), stop=(c == 7)),
                                        reads=["xnT", "Wv"], writes=[vreg])
                                op("VE", lambda e, tt=tt, half=half, dstp=dstp: e.tensor_tensor(
                                    v_sb[:, tt * 1024 + half * 512: tt * 1024 + (half + 1) * 512],
                                    dstp[:, :], bv_rep[:, half * 512:(half + 1) * 512], ALU.add),
                                    reads=[vreg, "bv"], writes=["v"])

                with ExitStack() as csc:
                    for R in ("Wo", "exp0", "exp1", "ctxn", "x1", "xn30", "xn31",
                              "xn3T", "attnT", "out0", "out1", "recip"):
                        T.barrier_region(R)
                    wo_sb = sb(f"wo_sb{b}", [128, 8192], BF16, csc)
                    exp_sb = sb(f"exp_sb{b}", [128, 2048], BF16, csc)
                    ctxn = sb(f"ctxn{b}", [128, 8 * 512], BF16, csc)
                    x1 = sb(f"x1{b}", [128, 4 * D], F32, csc)
                    xn3 = sb(f"xn3{b}", [128, 2 * D], BF16, csc)
                    xn3T = sb(f"xn3T{b}", [128, 8 * 512], BF16, csc)
                    attnT = sb(f"attnT{b}", [128, 8 * 512], BF16, csc)
                    out_sb = sb(f"out_sb{b}", [128, 2 * D], F32, csc)
                    recip_f = sb(f"recip_f{b}", [128, 1024], F32, csc)
                    wload([("Wo", wo_sb, wo_h)])

                    for st in range(NST):
                        nkt = 4 * st + 4
                        for hp in range(8):
                            def scores(j):
                                slot = j % 2
                                sregs = SCQ[2 * slot: 2 * slot + 2]
                                for h in range(2):
                                    bs = h * 64
                                    op("PE", lambda e, j=j, hp=hp, h=h, bs=bs, slot=slot:
                                       e.matmul(
                                        ps_sc[:, slot * 1024 + h * 512: slot * 1024 + (h + 1) * 512],
                                        kT[bs:bs + 64, hp * S + j * 128: hp * S + j * 128 + 128],
                                        qT[bs:bs + 64, hp * S + st * 512: hp * S + (st + 1) * 512],
                                        start=True, stop=True),
                                        reads=["kT", "qT"], writes=sregs)
                                op("AC", lambda e, slot=slot: e.activation(
                                    exp_sb[:, slot * 1024:(slot + 1) * 1024],
                                    ps_sc[:, slot * 1024:(slot + 1) * 1024],
                                    ACTF.Exp, scale=0.125),
                                    reads=sregs, writes=[f"exp{slot}"])
                                i = j - 4 * st
                                if i >= 0:
                                    op("VE", lambda e, slot=slot, i=i: e.tensor_tensor(
                                        exp_sb[:, slot * 1024:(slot + 1) * 1024],
                                        exp_sb[:, slot * 1024:(slot + 1) * 1024],
                                        bass.AP(masks, i * 512, [[2048, 128], [0, 2], [1, 512]]),
                                        ALU.mult),
                                        reads=[f"exp{slot}", "masks"], writes=[f"exp{slot}"])

                            def ctxsums(j):
                                slot = j % 2
                                first, last = (j == 0), (j == nkt - 1)
                                op("PE", lambda e, j=j, hp=hp, slot=slot, first=first, last=last:
                                   e.matmul(ps_a[0:64, 0:512],
                                            v_sb[:, j * 1024 + hp * 128: j * 1024 + hp * 128 + 64],
                                            exp_sb[:, slot * 1024: slot * 1024 + 512],
                                            start=first, stop=last, tile_position=(0, 0)),
                                   reads=["v", f"exp{slot}"], writes=["pa0"])
                                op("PE", lambda e, j=j, hp=hp, slot=slot, first=first, last=last:
                                   e.matmul(ps_a[0:64, 512:1024],
                                            v_sb[:, j * 1024 + hp * 128 + 64: j * 1024 + hp * 128 + 128],
                                            exp_sb[:, slot * 1024 + 512: slot * 1024 + 1024],
                                            start=first, stop=last),
                                   reads=["v", f"exp{slot}"], writes=["pa1"])
                                op("PE", lambda e, slot=slot, first=first, last=last:
                                   e.matmul(ps_c6[0:64, :], ones_sb[:, 0:64],
                                            exp_sb[:, slot * 1024: slot * 1024 + 512],
                                            start=first, stop=last),
                                   reads=["ones2", f"exp{slot}"], writes=["pc6"])
                                op("PE", lambda e, slot=slot, first=first, last=last:
                                   e.matmul(ps_c7[0:64, :], ones_sb[:, 0:64],
                                            exp_sb[:, slot * 1024 + 512: slot * 1024 + 1024],
                                            start=first, stop=last),
                                   reads=["ones2", f"exp{slot}"], writes=["pc7"])

                            scores(0)
                            for j in range(1, nkt):
                                scores(j)
                                ctxsums(j - 1)
                            ctxsums(nkt - 1)
                            op("VE", lambda e: e.reciprocal_approx_fast(
                                recip_f[0:64, 0:512], ps_c6[0:64, :]),
                                reads=["pc6"], writes=["recip"])
                            op("VE", lambda e: e.reciprocal_approx_fast(
                                recip_f[0:64, 512:1024], ps_c7[0:64, :]),
                                reads=["pc7"], writes=["recip"])
                            op("VE", lambda e, hp=hp: e.tensor_tensor(
                                ctxn[0:64, hp * 512:(hp + 1) * 512], ps_a[0:64, 0:512],
                                recip_f[0:64, 0:512], ALU.mult),
                                reads=["pa0", "recip"], writes=["ctxn"])
                            op("VE", lambda e, hp=hp: e.tensor_tensor(
                                ctxn[64:128, hp * 512:(hp + 1) * 512], ps_a[0:64, 512:1024],
                                recip_f[0:64, 512:1024], ALU.mult),
                                reads=["pa1", "recip"], writes=["ctxn"])

                        if dbg:
                            op("QA.dbg", lambda e, st=st, b=b: e.dma_start(
                                dbg_ctxn_h.ap()[:, (b * NST + st) * 4096:
                                                (b * NST + st + 1) * 4096],
                                ctxn[:, :]),
                                reads=["ctxn"])
                        def wo_mm(m):
                            slot = m % 2
                            for c in range(8):
                                op("PE", lambda e, m=m, c=c, slot=slot: e.matmul(
                                    ps_a[:, slot * 512:(slot + 1) * 512],
                                    wo_sb[:, c * 1024 + m * 128: c * 1024 + m * 128 + 128],
                                    ctxn[:, c * 512:(c + 1) * 512],
                                    start=(c == 0), stop=(c == 7)),
                                    reads=["Wo", "ctxn"], writes=[PA[slot]])
                            op("AC", lambda e, m=m, slot=slot: e.activation(
                                attnT[:, m * 512:(m + 1) * 512],
                                ps_a[:, slot * 512:(slot + 1) * 512],
                                ACTF.Identity, bias=bo_sb[:, m:m + 1]),
                                reads=[PA[slot], "bo"], writes=["attnT"])
                        for m in range(8):
                            wo_mm(m)
                        for p in range(2):
                            tls = [2 * p, 2 * p + 1]
                            for t in tls:
                                r0 = b * S + (st * 4 + t) * 128
                                op(f"QS.x{t % 2}", lambda e, t=t, r0=r0: e.dma_start(
                                    xt[:, (t % 2) * D:(t % 2) * D + D], x_ap[r0:r0 + 128, :]),
                                    writes=[f"xt{t % 2}"])
                            for t in tls:
                                regs = SCQ[0:2] if t % 2 == 0 else SCQ[2:4]
                                for m in range(8):
                                    op("PE", lambda e, m=m, t=t: e.matmul(
                                        ps_sc[:, (t % 2) * 1024 + m * 128:
                                              (t % 2) * 1024 + m * 128 + 128],
                                        attnT[:, m * 512 + t * 128: m * 512 + t * 128 + 128],
                                        ident[:, :], start=True, stop=True),
                                        reads=["attnT", "ident"], writes=regs)
                            srcs = [(ps_sc[:, (t % 2) * 1024:(t % 2) * 1024 + 512],
                                     ps_sc[:, (t % 2) * 1024 + 512:(t % 2) * 1024 + 1024])
                                    for t in tls]
                            ln_group(tls, srcs,
                                     [SCQ[0:2] if t % 2 == 0 else SCQ[2:4] for t in tls])
                            for t in tls:
                                regs = SCQ[0:2] if t % 2 == 0 else SCQ[2:4]
                                op("VE", lambda e, t=t: e.tensor_scalar(
                                    x1[:, t * D:(t + 1) * D],
                                    ps_sc[:, (t % 2) * 1024:(t % 2) * 1024 + 1024],
                                    mv[:, 2 * t:2 * t + 1], rs[:, t:t + 1],
                                    ALU.subtract, ALU.mult),
                                    reads=regs + [f"mv{t}", f"rs{t}"], writes=["x1"])
                                op("VE", lambda e, t=t: e.tensor_tensor(
                                    x1[:, t * D:(t + 1) * D], x1[:, t * D:(t + 1) * D],
                                    xt[:, (t % 2) * D:(t % 2) * D + D], ALU.add),
                                    reads=["x1", f"xt{t % 2}"], writes=["x1"])
                        if dbg:
                            for t in range(4):
                                op("QA.dbg", lambda e, t=t, st=st, b=b: e.dma_start(
                                    dbg_x1_h.ap()[b * S + (st * 4 + t) * 128:
                                                  b * S + (st * 4 + t) * 128 + 128, :],
                                    x1[:, t * D:(t + 1) * D]),
                                    reads=["x1"])
                        srcs = [(x1[:, t * D:t * D + 512], x1[:, t * D + 512:(t + 1) * D])
                                for t in range(4)]
                        ln_group([0, 1, 2, 3], srcs, [["x1"]] * 4)
                        for t in range(4):
                            op("VE", lambda e, t=t: e.tensor_scalar(
                                xn3[:, (t % 2) * D:(t % 2) * D + D], x1[:, t * D:(t + 1) * D],
                                mv[:, 2 * t:2 * t + 1], rs[:, t:t + 1], ALU.subtract, ALU.mult),
                                reads=["x1", f"mv{t}", f"rs{t}"], writes=[f"xn3{t % 2}"])
                            for c in range(8):
                                op("PE", lambda e, t=t, c=c: e.matmul(
                                    ps_a[:, c * 128:(c + 1) * 128],
                                    xn3[:, (t % 2) * D + c * 128:(t % 2) * D + c * 128 + 128],
                                    ident[:, :], start=True, stop=True),
                                    reads=[f"xn3{t % 2}", "ident"], writes=PA)
                            op("AC", lambda e, t=t: e.activation(
                                bass.AP(xn3T, t * 128, [[4096, 128], [512, 8], [1, 128]]),
                                ps_a[:, 0:1024], ACTF.Identity),
                                reads=PA, writes=["xn3T"])
                        for m in range(8):
                            slot = m % 2
                            vreg = "pc6" if slot == 0 else "pc7"
                            dstp = ps_c6 if slot == 0 else ps_c7
                            for c in range(8):
                                op("PE", lambda e, m=m, c=c, dstp=dstp: e.matmul(
                                    dstp[:, :],
                                    wsgnT[:, c * 1024 + m * 128: c * 1024 + m * 128 + 128],
                                    xn3T[:, c * 512:(c + 1) * 512],
                                    start=(c == 0), stop=(c == 7)),
                                    reads=["Wsgn", "xn3T"], writes=[vreg])
                            op("AC", lambda e, m=m, dstp=dstp: e.copy(
                                attnT[:, m * 512:(m + 1) * 512], dstp[:, :]),
                                reads=[vreg], writes=["attnT"])
                        for p in range(2):
                            tls = [2 * p, 2 * p + 1]
                            for t in tls:
                                regs = SCQ[0:2] if t % 2 == 0 else SCQ[2:4]
                                for m in range(8):
                                    op("PE", lambda e, m=m, t=t: e.matmul(
                                        ps_sc[:, (t % 2) * 1024 + m * 128:
                                              (t % 2) * 1024 + m * 128 + 128],
                                        attnT[:, m * 512 + t * 128: m * 512 + t * 128 + 128],
                                        ident[:, :], start=True, stop=True),
                                        reads=["attnT", "ident"], writes=regs)
                            srcs = [(ps_sc[:, (t % 2) * 1024:(t % 2) * 1024 + 512],
                                     ps_sc[:, (t % 2) * 1024 + 512:(t % 2) * 1024 + 1024])
                                    for t in tls]
                            ln_group(tls, srcs,
                                     [SCQ[0:2] if t % 2 == 0 else SCQ[2:4] for t in tls])
                            for t in tls:
                                regs = SCQ[0:2] if t % 2 == 0 else SCQ[2:4]
                                r0 = b * S + (st * 4 + t) * 128
                                op("VE", lambda e, t=t: e.tensor_scalar(
                                    out_sb[:, (t % 2) * D:(t % 2) * D + D],
                                    ps_sc[:, (t % 2) * 1024:(t % 2) * 1024 + 1024],
                                    mv[:, 2 * t:2 * t + 1], rs[:, t:t + 1],
                                    ALU.subtract, ALU.mult),
                                    reads=regs + [f"mv{t}", f"rs{t}"], writes=[f"out{t % 2}"])
                                op("VE", lambda e, t=t: e.tensor_tensor(
                                    out_sb[:, (t % 2) * D:(t % 2) * D + D],
                                    out_sb[:, (t % 2) * D:(t % 2) * D + D],
                                    x1[:, t * D:(t + 1) * D], ALU.add),
                                    reads=[f"out{t % 2}", "x1"], writes=[f"out{t % 2}"])
                                op(f"QA.o{t % 2}", lambda e, t=t, r0=r0: e.dma_start(
                                    y_ap[r0:r0 + 128, :],
                                    out_sb[:, (t % 2) * D:(t % 2) * D + D]),
                                    reads=[f"out{t % 2}"])

            def final_wait(e):
                for s in sorted(T.sem):
                    if s.startswith("QA."):
                        e.wait_ge(T.sem[s], T.cnt[s])
            block.sync(final_wait)
    nc.finalize()
    lower_extended_insts(nc)
    return nc


def _host_consts(S):
    ident = np.eye(128, dtype=ml_dtypes.bfloat16)
    masks = np.zeros((128, 4 * 512), dtype=np.float32)
    r = np.arange(128)[:, None]
    c = np.arange(512)[None, :]
    for i in range(4):
        masks[:, i * 512:(i + 1) * 512] = (c >= 128 * i + r)
    masks = masks.astype(ml_dtypes.bfloat16)
    inv = np.exp(np.arange(0, HD, 2, dtype=np.float32) * (-math.log(10000.0) / HD))
    pos = np.arange(S, dtype=np.float32)
    ang = pos[None, :] * inv[:, None]
    ang4 = np.tile(ang, (4, 1))
    ctab = np.cos(ang4).astype(ml_dtypes.bfloat16)
    stab = np.sin(ang4).astype(ml_dtypes.bfloat16)
    return ident, masks, ctab, stab


def _bias_tables(bq, bk, bo, bv):
    def plain(vec):
        return np.ascontiguousarray(np.asarray(vec, np.float32).reshape(8, 128).T)
    def shifted(vec):
        t = np.asarray(vec, np.float32).reshape(8, 128)
        p = np.arange(128)
        sp = np.where(p % 64 < 32, p + 32, p - 32)
        return np.ascontiguousarray(t[:, sp].T)
    bqt = np.ascontiguousarray(np.concatenate([plain(bq), shifted(bq)], axis=1))
    bkt = np.ascontiguousarray(np.concatenate([plain(bk), shifted(bk)], axis=1))
    bot = plain(bo)
    bvr = np.ascontiguousarray(np.broadcast_to(
        np.asarray(bv, np.float32)[None, :], (128, D)))
    return bqt, bkt, bot, bvr


def kernel(L0_attn_x, L0_attn_Wq, L0_attn_bq, L0_attn_Wk, L0_attn_bk,
           L0_attn_Wv, L0_attn_bv, L0_attn_Wo, L0_attn_bo, L0_ffn_weight):
    x = np.ascontiguousarray(np.asarray(L0_attn_x, dtype=np.float32))
    NB, S, _ = x.shape
    Bpc = NB // NCORES
    nc = build_program(Bpc, S)
    ident, masks, ctab, stab = _host_consts(S)
    bqt, bkt, bot, bvr = _bias_tables(L0_attn_bq, L0_attn_bk, L0_attn_bo, L0_attn_bv)
    common = dict(
        wq=np.ascontiguousarray(np.asarray(L0_attn_Wq, dtype=np.float32)),
        wk=np.ascontiguousarray(np.asarray(L0_attn_Wk, dtype=np.float32)),
        wv=np.ascontiguousarray(np.asarray(L0_attn_Wv, dtype=np.float32)),
        wo=np.ascontiguousarray(np.asarray(L0_attn_Wo, dtype=np.float32)),
        wf=np.ascontiguousarray(np.asarray(L0_ffn_weight, dtype=np.float32)),
        identt=ident, maskst=masks, ctabt=ctab, stabt=stab,
        bqt=bqt, bkt=bkt, bot=bot, bvr=bvr)
    in_maps = []
    for i in range(NCORES):
        m = dict(common)
        m["x"] = np.ascontiguousarray(x[i * Bpc:(i + 1) * Bpc].reshape(Bpc * S, D))
        in_maps.append(m)
    res = run_bass_kernel_spmd(nc, in_maps, core_ids=list(range(NCORES)))
    out = np.concatenate([np.asarray(r["y"], dtype=np.float32).reshape(Bpc, S, D)
                          for r in res.results], axis=0)
    return out
